# version 01 (final)
# baseline (speedup 1.0000x reference)
import math
from contextlib import ExitStack

import numpy as np
import ml_dtypes

import concourse.bass as bass
import concourse.mybir as mybir
from concourse.bass_utils import run_bass_kernel_spmd

F32 = mybir.dt.float32
BF16 = mybir.dt.bfloat16
AF = mybir.ActivationFunctionType
OP = mybir.AluOpType
AX = mybir.AxisListType.X
BF16NP = ml_dtypes.bfloat16

NCORE = 8
T = 1024
D = 2048
NT = T // 128
ND = D // 128

ATT0_SHIFT = 16.0
FFN1_CUT = 5

ENGINES = ("tensor", "vector", "scalar", "gpsimd", "sync")
CHANNELS = ("pe", "ve", "se", "gp", "dy", "dy0", "dy1", "dg", "cc",
            "ka", "kb", "va", "vb", "pa", "px", "ha", "hb")


class B:

    def __init__(self):
        nc = bass.Bass("TRN2", target_bir_lowering=False)
        self.nc = nc
        self.ctx = ExitStack()
        self.sem = {c: self.ctx.enter_context(nc.semaphore(c)) for c in CHANNELS}
        self.cnt = {c: 0 for c in CHANNELS}
        self.ops = {e: [] for e in ENGINES}
        self.waited = {(e, c): 0 for e in ENGINES for c in CHANNELS}
        self.arena_addr = None
        self.arena_size = 0
        self._carves = {}

    SERIAL = {"vector": "ve", "scalar": "se"}

    def op(self, engine, fn, ch=None, inc=0):
        sch = self.SERIAL.get(engine)
        if sch is not None:
            self.wait(engine, sch, self.cnt[sch])
            if ch is None:
                ch, inc = sch, 1
            assert ch == sch, (engine, ch)
        val = None
        if ch is not None and inc > 0:
            self.cnt[ch] += inc
            val = self.cnt[ch]
            sem = self.sem[ch]

            def g(eng, fn=fn, sem=sem, inc=inc):
                fn(eng).then_inc(sem, inc)
        else:
            def g(eng, fn=fn):
                fn(eng)
        self.ops[engine].append(g)
        return val

    def wait(self, engine, ch, val=None):
        if val is None:
            val = self.cnt[ch]
        if val <= 0 or val <= self.waited[(engine, ch)]:
            return
        self.waited[(engine, ch)] = val
        sem = self.sem[ch]
        self.ops[engine].append(lambda eng, sem=sem, val=val: eng.wait_ge(sem, val))

    def soft_barrier(self, engines=("tensor", "vector", "scalar")):
        for e in engines:
            for ch in CHANNELS:
                self.wait(e, ch)

    def barrier(self):
        for e, ch in (("tensor", "pe"), ("vector", "ve"), ("scalar", "se"),
                      ("gpsimd", "gp")):
            self.wait(e, ch)
            self.op(e, lambda eng: eng.nop(), ch=ch, inc=1)
        for e in ENGINES:
            for ch in CHANNELS:
                self.wait(e, ch)

    def alloc(self, name, shape, dtype):
        return self.nc.alloc_sbuf_tensor(name, shape, dtype)

    def arena_init(self, size_bytes):
        h = self.nc.alloc_sbuf_tensor("arena", [128, size_bytes], mybir.dt.uint8)
        self.arena_addr = self.nc.lookup_mloc(h).addr
        self.arena_size = size_bytes

    def carve(self, name, shape, dtype, at):
        per_part = int(np.prod(shape[1:])) * mybir.dt.size(dtype)
        assert at + per_part <= self.arena_size, (name, at, per_part)
        key = (name, tuple(shape), str(dtype), at)
        if key in self._carves:
            return self._carves[key]
        h = self.nc.alloc_sbuf_tensor_at(name, shape, dtype,
                                         offset=self.arena_addr + at)
        self._carves[key] = h
        return h

    def finalize(self):
        nc = self.nc
        with nc.Block() as block:
            @block.tensor
            def _(eng):
                for f in self.ops["tensor"]:
                    f(eng)

            @block.vector
            def _(eng):
                for f in self.ops["vector"]:
                    f(eng)

            @block.scalar
            def _(eng):
                for f in self.ops["scalar"]:
                    f(eng)

            @block.gpsimd
            def _(eng):
                for f in self.ops["gpsimd"]:
                    f(eng)

            @block.sync
            def _(eng):
                for f in self.ops["sync"]:
                    f(eng)
        self.ctx.close()
        return nc


class WStream:

    def __init__(self, b, stage, wbf):
        self.b = b
        self.stage = stage
        self.wbf = wbf
        self.cast_done = [0, 0]
        self.mm_done = [0, 0]
        self.dma_done = {}
        self.plan = []
        self.seq = 0
        self.cur = None

    def _prefetch(self):
        if not self.plan:
            return
        e = self.plan.pop(0)
        if not isinstance(e, dict):
            e = {"ap": e[0], "cols": e[1], "rows": (e[2] if len(e) > 2 else 128)}
        idx = self.seq + len(self.dma_done)
        b = self.b
        k = idx % 2
        ch = "dy0" if k == 0 else "dy1"
        b.wait("sync", "ve", self.cast_done[k])
        dy = b.op("sync",
                  lambda eng, o=self.stage[k], i=e["ap"], r=e["rows"],
                  c=e["cols"]:
                  eng.dma_start(out=o[0:r, 0:c], in_=i),
                  ch=ch, inc=16)
        e["ch"], e["dy"] = ch, dy
        self.dma_done[idx] = e

    def begin(self, plan):
        assert not self.plan and not self.dma_done and self.cur is None
        self.plan = list(plan)
        self._prefetch()
        self._prefetch()

    def get(self, war=None):
        assert self.cur is None
        idx = self.seq
        b = self.b
        k = idx % 2
        e = self.dma_done.pop(idx)
        dst = e.get("dst")
        b.wait("vector", e["ch"], e["dy"])
        if dst is None:
            b.wait("vector", "pe", self.mm_done[k])
            tgt = self.wbf[k]
        else:
            tgt = dst
            if war is not None:
                b.wait("vector", war[0], war[1])
        self.cast_done[k] = b.op(
            "vector",
            lambda eng, o=tgt, i=self.stage[k], r=e["rows"], c=e["cols"]:
            eng.tensor_copy(o[0:r, 0:c], i[0:r, 0:c]),
            ch="ve", inc=1)
        b.wait("tensor", "ve", self.cast_done[k])
        self.cur = (idx, dst is None)
        return tgt

    def done(self):
        idx, is_wbf = self.cur
        if is_wbf:
            self.mm_done[idx % 2] = self.b.cnt["pe"]
        self.cur = None
        self.seq += 1
        self._prefetch()



def _chunk_cols(w):
    d, f = w.shape
    n = d // 128
    return np.ascontiguousarray(
        w.reshape(n, 128, f).transpose(1, 0, 2).reshape(128, n * f))


def _stack_rows(blocks):
    return np.ascontiguousarray(np.concatenate(blocks, axis=0))


def host_layouts(inputs):
    L = {}
    f32 = np.float32

    L["ident"] = np.eye(128, dtype=np.float32).astype(BF16NP)
    L["ones_row"] = np.ones((1, 128), dtype=BF16NP)
    sel8 = np.zeros((8, 8, 128), dtype=np.float32)
    for e in range(8):
        sel8[e, e, :] = 1.0
    L["sel8"] = np.ascontiguousarray(sel8.transpose(1, 0, 2).reshape(8, 1024)
                                     ).astype(BF16NP)
    L["ones_col"] = np.ones((128, 1), dtype=BF16NP)

    hd = 128
    freqs = (np.float32(1.0) / (np.float32(10000.0) **
             (np.arange(0, hd, 2, dtype=f32) / f32(hd)))).astype(f32)
    t_all = np.arange(NCORE * T, dtype=f32)
    f = np.outer(t_all, freqs).astype(f32)
    sin = np.sin(f).astype(f32)
    cos = np.cos(f).astype(f32)
    cos_fm = np.repeat(cos, 2, axis=1).T
    ssin = np.empty((hd, NCORE * T), dtype=f32)
    ssin[0::2] = -sin.T
    ssin[1::2] = sin.T
    L["_rope_cos_all"] = np.ascontiguousarray(cos_fm)
    L["_rope_ssin_all"] = np.ascontiguousarray(ssin)

    L["wg_r"] = _chunk_cols(np.asarray(inputs["L0_ffn_w_gate"], f32))
    wgu = np.asarray(inputs["L0_ffn_w_gate_up"], f32)
    blocks = []
    for e in range(8):
        g, u = wgu[e][:, :1024], wgu[e][:, 1024:]
        for k in range(8):
            pair = np.concatenate([g[:, k * 128:(k + 1) * 128],
                                   u[:, k * 128:(k + 1) * 128]], 1)
            blocks.append(_chunk_cols(pair))
    L["ffn0_wgu"] = _stack_rows(blocks)

    wd = np.asarray(inputs["L0_ffn_w_down"], f32)
    blocks = []
    for e in range(8):
        for fb in range(4):
            blocks.append(_chunk_cols(wd[e][:, fb * 512:(fb + 1) * 512]))
    L["ffn0_wd"] = _stack_rows(blocks)

    wshgu = np.asarray(inputs["L0_ffn_w_sh_gate_up"], f32)
    g, u = wshgu[:, :4096], wshgu[:, 4096:]
    blocks = []
    for k in range(32):
        pair = np.concatenate([g[:, k * 128:(k + 1) * 128],
                               u[:, k * 128:(k + 1) * 128]], 1)
        blocks.append(_chunk_cols(pair))
    L["ffn0_wshgu"] = _stack_rows(blocks)

    wshd = np.asarray(inputs["L0_ffn_w_sh_down"], f32)
    L["ffn0_wshd"] = _stack_rows(
        [_chunk_cols(wshd[:, fb * 512:(fb + 1) * 512]) for fb in range(4)])

    wqkv = np.asarray(inputs["L0_attn_w_qkv"], f32)
    wq, wk, wv = wqkv[:, :2048], wqkv[:, 2048:4096], wqkv[:, 4096:]
    L["a0_wq"] = _stack_rows([_chunk_cols(wq[:, ft * 128:(ft + 1) * 128])
                              for ft in range(16)])
    L["a0_wk"] = _stack_rows([_chunk_cols(wk[:, ft * 128:(ft + 1) * 128])
                              for ft in range(16)])
    L["a0_wv"] = _stack_rows([_chunk_cols(wv[:, j * 256:(j + 1) * 256])
                              for j in range(8)])
    wp = np.asarray(inputs["L0_attn_w_proj"], f32)
    L["a0_wp"] = _stack_rows([_chunk_cols(wp[:, j * 256:(j + 1) * 256])
                              for j in range(8)])

    wqg = np.asarray(inputs["L1_attn_Wqg"], f32)
    wkv = np.asarray(inputs["L1_attn_Wkv"], f32)
    qcols = np.concatenate([wqg[:, h * 192:h * 192 + 64] for h in range(16)], 1)
    gcols = np.concatenate([wqg[:, h * 192 + 64:h * 192 + 192] for h in range(16)], 1)
    kcols = np.concatenate([wkv[:, h * 192:h * 192 + 64] for h in range(16)], 1)
    vcols = np.concatenate([wkv[:, h * 192 + 64:h * 192 + 192] for h in range(16)], 1)
    L["a1_wq"] = _stack_rows([_chunk_cols(qcols[:, j * 256:(j + 1) * 256])
                              for j in range(4)])
    L["a1_wk"] = _stack_rows([_chunk_cols(kcols[:, j * 256:(j + 1) * 256])
                              for j in range(4)])
    L["a1_wv"] = _stack_rows([_chunk_cols(vcols[:, j * 256:(j + 1) * 256])
                              for j in range(8)])
    L["a1_wg"] = _stack_rows([_chunk_cols(gcols[:, ft * 128:(ft + 1) * 128])
                              for ft in range(16)])
    wo = np.asarray(inputs["L1_attn_Wo"], f32)
    L["a1_wo"] = _stack_rows([_chunk_cols(wo[:, j * 256:(j + 1) * 256])
                              for j in range(8)])
    L["a1_ls"] = np.ascontiguousarray(
        np.stack([np.asarray(inputs["L1_attn_ls_iaw"], f32),
                  np.asarray(inputs["L1_attn_ls_hws"], f32)], 0))

    L["f1_confw"] = _chunk_cols(
        np.ascontiguousarray(np.asarray(inputs["L1_ffn_conf_W"], f32).T))
    L["f1_small"] = np.ascontiguousarray(
        np.stack([np.asarray(inputs["L1_ffn_conf_b"], f32),
                  np.asarray(inputs["L1_ffn_expert_wealth"], f32)], 0))

    bg = np.asarray(inputs["L1_ffn_base_gate"], f32)
    bu = np.asarray(inputs["L1_ffn_base_up"], f32)
    L["f1_bg"] = _stack_rows([_chunk_cols(bg[:, ib * 256:(ib + 1) * 256])
                              for ib in range(28)])
    L["f1_bu"] = _stack_rows([_chunk_cols(bu[:, ib * 256:(ib + 1) * 256])
                              for ib in range(28)])
    bd = np.asarray(inputs["L1_ffn_base_down"], f32)
    L["f1_bd"] = _stack_rows([_chunk_cols(bd[:, fb * 512:(fb + 1) * 512])
                              for fb in range(4)])

    gA = np.asarray(inputs["L1_ffn_gate_A"], f32)
    uA = np.asarray(inputs["L1_ffn_up_A"], f32)

    def layA(a):
        return np.ascontiguousarray(
            a.transpose(1, 0, 2).reshape(16, 128, 512)
            .transpose(1, 0, 2).reshape(128, 16 * 512))
    L["f1_gA"] = layA(gA)
    L["f1_uA"] = layA(uA)
    gB = np.asarray(inputs["L1_ffn_gate_B"], f32)
    uB = np.asarray(inputs["L1_ffn_up_B"], f32)
    L["f1_gB"] = np.ascontiguousarray(gB.transpose(1, 0, 2).reshape(64, 8 * 7168))
    L["f1_uB"] = np.ascontiguousarray(uB.transpose(1, 0, 2).reshape(64, 8 * 7168))
    dA = np.asarray(inputs["L1_ffn_down_A"], f32)
    L["f1_dA"] = np.ascontiguousarray(
        dA.reshape(8, 56, 128, 64).transpose(2, 0, 1, 3).reshape(128, 8 * 56 * 64))
    dB = np.asarray(inputs["L1_ffn_down_B"], f32)
    L["f1_dB"] = np.ascontiguousarray(dB.transpose(1, 0, 2).reshape(64, 8 * 2048))
    return L



class MK:
    A_X = 0
    A_HT = 64 * 1024
    A_WST = 96 * 1024
    A_WBF = 128 * 1024
    A_FREE = 144 * 1024
    A_END = 180 * 1024

    def __init__(self, b):
        self.b = b

        self.sg = b.alloc("sg", [128, 512], F32)
        self.t1 = b.alloc("t1", [128, 512], F32)
        self.t2 = b.alloc("t2", [128, 512], F32)
        self.pbuf = [b.alloc(f"pbuf{k}", [128, 512], BF16) for k in (0, 1)]
        self.stat = b.alloc("stat", [128, 16], F32)
        self.r8a = b.alloc("r8a", [128, 8], F32)
        self.r8b = b.alloc("r8b", [128, 8], F32)
        self.cmb = b.alloc("cmb", [128, 64], F32)
        self.chalf = b.alloc("chalf", [128, 64], F32)
        self.row1 = b.alloc("row1", [1, 512], F32)
        self.row2 = b.alloc("row2", [1, 512], F32)
        self.ident = b.alloc("ident_sb", [128, 128], BF16)
        self.ones_row = b.alloc("ones_row_sb", [1, 128], BF16)
        self.ones_col = b.alloc("ones_col_sb", [128, 1], BF16)
        self.rope_cos = b.alloc("rope_cos_sb", [128, 1024], F32)
        self.rope_ssin = b.alloc("rope_ssin_sb", [128, 1024], F32)
        self.wg_bf = b.alloc("wg_bf", [128, 128], BF16)
        self.rowb = b.alloc("rowb", [1, 512], BF16)
        self.confw_bf = b.alloc("confw_bf", [128, 128], BF16)
        self.onesf = b.alloc("onesf", [1, 128], F32)
        self.weT = b.alloc("weT", [8, 1024], BF16)
        self.sel8 = b.alloc("sel8_sb", [8, 1024], BF16)
        self.cmb_bf = b.alloc("cmb_bf", [128, 64], BF16)
        self.small = b.alloc("small_sb", [128, 64], F32)

        b.arena_init(self.A_END)
        self.x = b.carve("x_sb", [128, NT * D], F32, self.A_X)
        self.hT = b.carve("hT", [128, ND * T], BF16, self.A_HT)
        self.wstage = [b.carve(f"wstage{k}", [128, 4096], F32,
                               self.A_WST + k * 16384) for k in (0, 1)]
        self.wbf = [b.carve(f"wbf{k}", [128, 4096], BF16,
                            self.A_WBF + k * 8192) for k in (0, 1)]
        self.ws = WStream(b, self.wstage, self.wbf)

        nc = b.nc
        self.pb = [nc.alloc_psum_tensor(f"pb{i}", [128, 512], F32)
                   for i in range(6)]
        self.pt = nc.alloc_psum_tensor("pt", [128, 2048], BF16)
        self.bank_free = [("ve", 0)] * 6

        self.dram = {}
        self.input_names = set()

    def din(self, name, shape, dtype=F32):
        if name not in self.dram:
            self.dram[name] = self.b.nc.dram_tensor(name, shape, dtype,
                                                    kind="ExternalInput")
            self.input_names.add(name)
        return self.dram[name]

    def dout(self, name, shape, dtype=F32):
        if name not in self.dram:
            self.dram[name] = self.b.nc.dram_tensor(name, shape, dtype,
                                                    kind="ExternalOutput")
        return self.dram[name]

    def dint(self, name, shape, dtype=F32, addr_space="Local"):
        if name not in self.dram:
            self.dram[name] = self.b.nc.dram_tensor(name, shape, dtype,
                                                    addr_space=addr_space)
        return self.dram[name]

    def hT_tile(self, c, lo, n):
        return self.hT[:, c * T + lo: c * T + lo + n]

    def hT_dest(self, tt):
        return bass.AP(self.hT, tt * 128, [[ND * T, 128], [T, ND], [1, 128]])

    def pt_src(self):
        return bass.AP(self.pt, 0, [[2048, 128], [128, ND], [1, 128]])

    def bank_acquire(self, i):
        ch, v = self.bank_free[i]
        self.b.wait("tensor", ch, v)

    def bank_release(self, i, ch):
        self.bank_free[i] = (ch, self.b.cnt[ch])

    def emit_prolog(self, load_x=True):
        b = self.b
        if load_x:
            x_in = self.din("x_in", [T, D])
            for tt in range(NT):
                b.op("sync", lambda eng, tt=tt: eng.dma_start(
                    out=self.x[:, tt * D:(tt + 1) * D],
                    in_=x_in[tt * 128:(tt + 1) * 128, :]), ch="dy", inc=16)
        for nm, h in (("ident", self.ident), ("ones_row", self.ones_row),
                      ("ones_col", self.ones_col)):
            d = self.din(nm, list(h.shape), BF16)
            b.op("sync", lambda eng, h=h, d=d: eng.dma_start(
                out=h[:, :], in_=d[:, :]), ch="dy", inc=16)
        for nm, h in (("rope_cos", self.rope_cos), ("rope_ssin", self.rope_ssin)):
            d = self.din(nm, [128, 1024], F32)
            b.op("sync", lambda eng, h=h, d=d: eng.dma_start(
                out=h[:, :], in_=d[:, :]), ch="dy", inc=16)
        d = self.din("sel8", [8, 1024], BF16)
        b.op("sync", lambda eng, d=d: eng.dma_start(
            out=self.sel8[:, :], in_=d[:, :]), ch="dy", inc=16)
        b.op("vector", lambda eng: eng.memset(self.onesf[:, :], 1.0))
        wg = self.din("wg_r", [128, 128], F32)
        b.op("sync", lambda eng, wg=wg: eng.dma_start(
            out=self.wstage[0][:, 0:128], in_=wg[:, :]), ch="dy", inc=16)
        b.wait("vector", "dy")
        self.ws.cast_done[0] = b.op("vector", lambda eng: eng.tensor_copy(
            self.wg_bf[:, :], self.wstage[0][:, 0:128]), ch="ve", inc=1)
        b.wait("tensor", "ve")
        b.wait("tensor", "dy")
        b.wait("scalar", "dy")

    def emit_epilog(self):
        b = self.b
        x_out = self.dout("x_out", [T, D])
        b.barrier()
        for tt in range(NT):
            b.op("gpsimd", lambda eng, tt=tt: eng.dma_start(
                out=x_out[tt * 128:(tt + 1) * 128, :],
                in_=self.x[:, tt * D:(tt + 1) * D]), ch="dg", inc=16)
        b.wait("gpsimd", "dg")

    def emit_ln(self, tag):
        b = self.b
        b.soft_barrier()
        h_bf = b.carve("h_bf", [128, D], BF16, self.A_FREE + 32 * 1024)
        mean = self.stat[:, 0:1]
        ssq = self.stat[:, 1:2]
        var = self.stat[:, 2:3]
        std = self.stat[:, 3:4]
        rstd = self.stat[:, 4:5]
        m2 = self.stat[:, 5:6]
        for tt in range(NT):
            xt = self.x[:, tt * D:(tt + 1) * D]
            b.op("vector", lambda eng, xt=xt, o=mean: eng.tensor_reduce(
                o, xt, AX, OP.add))
            b.op("vector", lambda eng, o=mean: eng.tensor_scalar_mul(
                o, o, 1.0 / D), ch="ve", inc=1)
            b.wait("scalar", "pe")
            b.wait("scalar", "ve")
            b.op("scalar", lambda eng, xt=xt, o=h_bf, a=ssq: eng.activation(
                out=o[:, :], in_=xt, func=AF.Square, accum_out=a), ch="se", inc=1)
            b.wait("vector", "se")
            b.op("vector", lambda eng, m=mean, o=m2: eng.tensor_mul(o, m, m))
            b.op("vector", lambda eng, s=ssq, o=var, m=m2: eng.tensor_scalar(
                o, s, 1.0 / D, None, OP.mult))
            b.op("vector", lambda eng, o=var, m=m2: eng.tensor_sub(o, o, m),
                 ch="ve", inc=1)
            b.wait("scalar", "ve")
            b.op("scalar", lambda eng, v=var, o=std: eng.activation(
                out=o, in_=v, func=AF.Sqrt, bias=np.float32(1e-5)), ch="se", inc=1)
            b.wait("vector", "se")
            b.op("vector", lambda eng, s=std, o=rstd: eng.reciprocal(o, s))
            b.wait("vector", "pe")
            b.op("vector", lambda eng, xt=xt, o=h_bf, m=mean, r=rstd:
                 eng.tensor_scalar(o[:, :], xt, m, r, OP.subtract, OP.mult),
                 ch="ve", inc=1)
            b.wait("tensor", "ve")
            for c in range(ND):
                b.op("tensor", lambda eng, c=c: eng.transpose(
                    self.pt[:, c * 128:(c + 1) * 128],
                    h_bf[:, c * 128:(c + 1) * 128], self.ident[:, :]),
                    ch="pe", inc=(1 if c == ND - 1 else 0))
            b.wait("vector", "pe")
            b.op("vector", lambda eng, tt=tt: eng.tensor_copy(
                self.hT_dest(tt), self.pt_src()), ch="ve", inc=1)

    def emit_ffn0(self, tag):
        b = self.b
        ws = self.ws
        self.emit_ln(tag)

        wshgu = self.din("ffn0_wshgu", [4096, 4096])
        wshd = self.din("ffn0_wshd", [512, 16384])
        wgu_d = self.din("ffn0_wgu", [8192, 4096])
        wd_d = self.din("ffn0_wd", [4096, 4096])

        plan = []
        for s in range(2):
            for k in range(16):
                kg = s * 16 + k
                plan.append((wshgu[kg * 128:(kg + 1) * 128, :], 4096))
            for fb in range(4):
                for icg in range(4):
                    plan.append((wshd[fb * 128:(fb + 1) * 128,
                                      (s * 4 + icg) * 2048:
                                      (s * 4 + icg + 1) * 2048], 2048))
        for e in range(8):
            for k in range(8):
                r = e * 8 + k
                plan.append((wgu_d[r * 128:(r + 1) * 128, :], 4096))
            for fb in range(4):
                r = e * 4 + fb
                plan.append((wd_d[r * 128:(r + 1) * 128, :], 4096))
        ws.begin(plan)

        b.wait("tensor", "ve")
        self.bank_acquire(5)
        for tt in range(NT):
            for c in range(ND):
                b.op("tensor", lambda eng, tt=tt, c=c: eng.matmul(
                    self.pb[5][:, tt * 8:(tt + 1) * 8],
                    self.hT_tile(c, tt * 128, 128),
                    self.wg_bf[:, c * 8:(c + 1) * 8],
                    start=(c == 0), stop=(c == ND - 1)),
                    ch="pe", inc=(1 if (tt == NT - 1 and c == ND - 1) else 0))
        b.wait("vector", "pe")
        for tt in range(NT):
            lg = self.pb[5][:, tt * 8:(tt + 1) * 8]
            m1 = self.stat[:, 8:9]
            m2 = self.stat[:, 9:10]
            nm1 = self.stat[:, 10:11]
            e2 = self.stat[:, 11:12]
            den = self.stat[:, 12:13]
            w1 = self.stat[:, 13:14]
            w2 = self.stat[:, 14:15]
            b.op("vector", lambda eng, lg=lg, o=m1: eng.tensor_reduce(
                o, lg, AX, OP.max))
            b.op("vector", lambda eng, lg=lg, m=m1: eng.tensor_scalar(
                self.r8a[:, :], lg, m, None, OP.is_equal))
            b.op("vector", lambda eng, lg=lg: eng.scalar_tensor_tensor(
                self.r8b[:, :], self.r8a[:, :], -1e30, lg, OP.mult, OP.add))
            b.op("vector", lambda eng, o=m2: eng.tensor_reduce(
                o, self.r8b[:, :], AX, OP.max))
            b.op("vector", lambda eng, m=m2: eng.tensor_scalar(
                self.r8b[:, :], self.r8b[:, :], m, None, OP.is_equal))
            b.op("vector", lambda eng, m=m1, o=nm1: eng.tensor_scalar_mul(
                o, m, -1.0))
            b.wait("scalar", "ve")
            b.op("scalar", lambda eng, m=m2, o=e2, bb=nm1: eng.activation(
                out=o, in_=m, func=AF.Exp, bias=bb))
            b.wait("vector", "se")
            b.op("vector", lambda eng, i=e2, o=den: eng.tensor_scalar(
                o, i, 1.0, None, OP.add))
            b.op("vector", lambda eng, i=den, o=w1: eng.reciprocal(o, i))
            b.op("vector", lambda eng, o=w2, a=e2, c=w1: eng.tensor_mul(o, a, c))
            b.op("vector", lambda eng, w=w1: eng.tensor_scalar(
                self.r8a[:, :], self.r8a[:, :], w, None, OP.mult))
            b.op("vector", lambda eng, tt=tt, w=w2: eng.scalar_tensor_tensor(
                self.cmb[:, tt * 8:(tt + 1) * 8], self.r8b[:, :], w,
                self.r8a[:, :], OP.mult, OP.add))
        b.op("vector", lambda eng: eng.tensor_scalar_mul(
            self.chalf[:, :], self.cmb[:, :], 0.5))
        self.bank_release(5, "ve")

        act_sh = b.carve("act_sh", [128, 16 * 1024], BF16, self.A_FREE)

        def gu_phase(nk, act):
            for k in range(nk):
                w = ws.get()
                for th in range(2):
                    j = k * 2 + th
                    bg_, bu_ = (2 * j) % 4, (2 * j + 1) % 4
                    self.bank_acquire(bg_)
                    for c in range(ND):
                        b.op("tensor", lambda eng, c=c, th=th, w=w, bk=bg_:
                             eng.matmul(
                                 self.pb[bk][:, :], w[:, c * 256:c * 256 + 128],
                                 self.hT_tile(c, th * 512, 512),
                                 start=(c == 0), stop=(c == ND - 1)),
                             ch="pe", inc=(1 if c == ND - 1 else 0))
                    pe_g = b.cnt["pe"]
                    self.bank_acquire(bu_)
                    for c in range(ND):
                        b.op("tensor", lambda eng, c=c, th=th, w=w, bk=bu_:
                             eng.matmul(
                                 self.pb[bk][:, :],
                                 w[:, c * 256 + 128:c * 256 + 256],
                                 self.hT_tile(c, th * 512, 512),
                                 start=(c == 0), stop=(c == ND - 1)),
                             ch="pe", inc=(1 if c == ND - 1 else 0))
                    pe_u = b.cnt["pe"]
                    b.wait("scalar", "pe", pe_g)
                    b.wait("scalar", "ve")
                    b.op("scalar", lambda eng, bk=bg_: eng.activation(
                        out=self.sg[:, :], in_=self.pb[bk][:, :], func=AF.Silu))
                    self.bank_release(bg_, "se")
                    b.wait("vector", "pe", pe_u)
                    b.wait("vector", "se")
                    b.op("vector", lambda eng, k=k, th=th, bk=bu_, act=act:
                         eng.tensor_mul(
                             act[:, k * 1024 + th * 512:
                                 k * 1024 + th * 512 + 512],
                             self.sg[:, :], self.pb[bk][:, :]))
                    self.bank_release(bu_, "ve")
                ws.done()

        def down_phase(nic_blk, nic_mm, act, iblk_fn, fb_fn, scalar_fn):
            for blk in range(nic_blk):
                w = ws.get()
                for tt in range(NT):
                    bk = 4 + tt % 2
                    self.bank_acquire(bk)
                    for ic in range(nic_mm):
                        ib = iblk_fn(blk, ic)
                        b.op("tensor",
                             lambda eng, ic=ic, tt=tt, ib=ib, w=w, bk=bk:
                             eng.matmul(
                                 self.pb[bk][:, :],
                                 act[:, ib * 1024 + tt * 128:
                                     ib * 1024 + tt * 128 + 128],
                                 w[:, ic * 512:(ic + 1) * 512],
                                 start=(ic == 0), stop=(ic == nic_mm - 1)),
                             ch="pe", inc=(1 if ic == nic_mm - 1 else 0))
                    b.wait("vector", "pe")
                    fb = fb_fn(blk)
                    b.op("vector", lambda eng, tt=tt, fb=fb, bk=bk,
                         sc=scalar_fn(tt):
                         eng.scalar_tensor_tensor(
                             self.x[:, tt * D + fb * 512:
                                    tt * D + fb * 512 + 512],
                             self.pb[bk][:, :], sc,
                             self.x[:, tt * D + fb * 512:
                                    tt * D + fb * 512 + 512],
                             OP.mult, OP.add))
                    self.bank_release(bk, "ve")
                ws.done()

        for s in range(2):
            gu_phase(16, act_sh)
            down_phase(16, 4, act_sh,
                       lambda blk, ic: (blk % 4) * 4 + ic,
                       lambda blk: blk // 4,
                       lambda tt: 0.5)

        act_e = b.carve("act_e", [128, 8 * 1024], BF16, self.A_FREE)
        for e in range(8):
            gu_phase(8, act_e)
            down_phase(4, 8, act_e,
                       lambda blk, ic: ic,
                       lambda blk: blk,
                       lambda tt, e=e: self.chalf[:, tt * 8 + e: tt * 8 + e + 1])

        assert not ws.plan and not ws.dma_done


    def emit_att0(self, tag, mode="fused"):
        b = self.b
        ws = self.ws
        pre = mode in ("fused", "pre")
        post = mode in ("fused", "post")
        if pre:
            self.emit_ln(tag)

        if pre:
            wq_d = self.din("a0_wq", [2048, 2048])
            wk_d = self.din("a0_wk", [2048, 2048])
            wv_d = self.din("a0_wv", [1024, 4096])
        if post:
            wp_d = self.din("a0_wp", [1024, 4096])
        if mode == "fused":
            k_loc = self.dint("a0k_loc", [2048, 1024], BF16)
            k_all = self.dint("a0k_all", [16384, 1024], BF16)
            v_loc = self.dint("a0v_loc", [1024, 2048], BF16)
            v_all = self.dint("a0v_all", [8192, 2048], BF16)
        elif mode == "pre":
            k_loc = self.dout("a0k_loc", [2048, 1024], BF16)
            v_loc = self.dout("a0v_loc", [1024, 2048], BF16)
            qt_io = self.dout("a0q_dump", [128, 16384], BF16)
        else:
            k_all = self.din("a0k_all", [16384, 1024], BF16)
            v_all = self.din("a0v_all", [8192, 2048], BF16)
            qt_io = self.din("a0q_dump", [128, 16384], BF16)
        xsp = self.dint("xspill", [T, D], F32)

        QT = b.carve("a0_QT", [128, 16 * 1024], BF16, self.A_FREE)
        OT = b.carve("a0_OT", [128, 16 * 1024], BF16, self.A_HT)
        kbuf = [b.carve(f"a0_kb{k}", [128, 8192], BF16,
                        self.A_X + k * 16384) for k in (0, 1)]
        vbuf = [b.carve(f"a0_vb{k}", [128, 8192], BF16,
                        self.A_X + 32768 + k * 16384) for k in (0, 1)]

        if post:
            b.wait("gpsimd", "ve")
            b.wait("gpsimd", "dy")
            for tt in range(NT):
                b.op("gpsimd", lambda eng, tt=tt: eng.dma_start(
                    out=xsp[tt * 128:(tt + 1) * 128, :],
                    in_=self.x[:, tt * D:(tt + 1) * D]), ch="dg", inc=16)

        plan = []
        if pre:
            plan += [(wq_d[h * 128:(h + 1) * 128, :], 2048) for h in range(16)]
            plan += [(wk_d[h * 128:(h + 1) * 128, :], 2048) for h in range(16)]
            plan += [(wv_d[j * 128:(j + 1) * 128, :], 4096) for j in range(8)]
        if post:
            plan += [(wp_d[j * 128:(j + 1) * 128, :], 4096) for j in range(8)]
        ws.begin(plan)
        if mode == "post":
            b.wait("sync", "ve")
            b.wait("sync", "pe")
            b.op("sync", lambda eng: eng.dma_start(
                out=QT[:, :], in_=qt_io[:, :]), ch="dy", inc=16)
            b.wait("tensor", "dy")

        pairswap = []
        for i in range(16):
            pairswap += [2 * i + 1, 2 * i]
        pbuf_free = [0, 0]

        for is_k in ((0, 1) if pre else ()):
            for h in range(16):
                w = ws.get()
                for th in range(2):
                    bk = (h * 2 + th) % 2
                    self.bank_acquire(bk)
                    for c in range(ND):
                        b.op("tensor", lambda eng, c=c, th=th, w=w, bk=bk:
                             eng.matmul(
                                 self.pb[bk][:, :], w[:, c * 128:(c + 1) * 128],
                                 self.hT_tile(c, th * 512, 512),
                                 start=(c == 0), stop=(c == ND - 1)),
                             ch="pe", inc=(1 if c == ND - 1 else 0))
                    b.wait("scalar", "pe")
                    b.wait("scalar", "ve")
                    b.op("scalar", lambda eng, bk=bk: eng.activation(
                        out=self.t1[:, :], in_=self.pb[bk][:, :], func=AF.Copy))
                    self.bank_release(bk, "se")
                    b.wait("vector", "se")
                    b.op("vector", lambda eng: eng.stream_shuffle(
                        self.t2[:, :], self.t1[:, :], pairswap))
                    b.op("vector", lambda eng, th=th: eng.tensor_mul(
                        self.t2[:, :], self.t2[:, :],
                        self.rope_ssin[:, th * 512:(th + 1) * 512]))
                    b.op("vector", lambda eng, th=th: eng.tensor_mul(
                        self.t1[:, :], self.t1[:, :],
                        self.rope_cos[:, th * 512:(th + 1) * 512]))
                    if not is_k:
                        b.op("vector", lambda eng, h=h, th=th: eng.tensor_add(
                            QT[:, h * 1024 + th * 512: h * 1024 + th * 512 + 512],
                            self.t1[:, :], self.t2[:, :]))
                    else:
                        b.wait("vector", "pa" if th == 0 else "px",
                               pbuf_free[th])
                        b.op("vector", lambda eng, th=th: eng.tensor_add(
                            self.pbuf[th][:, :], self.t1[:, :], self.t2[:, :]))
                        b.wait("gpsimd", "ve")
                        pbuf_free[th] = b.op(
                            "gpsimd", lambda eng, h=h, th=th: eng.dma_start(
                                out=k_loc[h * 128:(h + 1) * 128,
                                          th * 512:(th + 1) * 512],
                                in_=self.pbuf[th][:, :]),
                            ch="pa" if th == 0 else "px", inc=16)
                ws.done()

        for j in (range(8) if pre else ()):
            w = ws.get()
            for tt in range(NT):
                bk = 2 + tt % 2
                self.bank_acquire(bk)
                for c in range(ND):
                    b.op("tensor", lambda eng, c=c, tt=tt, w=w, bk=bk:
                         eng.matmul(
                             self.pb[bk][:, 0:256], self.hT_tile(c, tt * 128, 128),
                             w[:, c * 256:(c + 1) * 256],
                             start=(c == 0), stop=(c == ND - 1)),
                         ch="pe", inc=(1 if c == ND - 1 else 0))
                kk = tt % 2
                b.wait("vector", "pe")
                b.wait("vector", "pa" if kk == 0 else "px", pbuf_free[kk])
                b.op("vector", lambda eng, bk=bk, kk=kk: eng.tensor_copy(
                    self.pbuf[kk][:, 0:256], self.pb[bk][:, 0:256]))
                self.bank_release(bk, "ve")
                b.wait("gpsimd", "ve")
                pbuf_free[kk] = b.op(
                    "gpsimd", lambda eng, j=j, tt=tt, kk=kk: eng.dma_start(
                        out=v_loc[tt * 128:(tt + 1) * 128,
                                  j * 256:(j + 1) * 256],
                        in_=self.pbuf[kk][:, 0:256]),
                    ch="pa" if kk == 0 else "px", inc=16)
            ws.done()

        if mode == "pre":
            b.wait("gpsimd", "ve")
            b.op("gpsimd", lambda eng: eng.dma_start(
                out=qt_io[:, :], in_=QT[:, :]), ch="dg", inc=16)
            b.wait("gpsimd", "dg")
            b.wait("gpsimd", "pa")
            b.wait("gpsimd", "px")
            return
        if mode == "fused":
            b.wait("gpsimd", "dg")
            b.wait("gpsimd", "pa")
            b.wait("gpsimd", "px")
            b.op("gpsimd", lambda eng: eng.collective_compute(
                "AllGather", OP.bypass, replica_groups=[list(range(NCORE))],
                ins=[k_loc.ap().opt()], outs=[k_all.ap().opt()]),
                ch="cc", inc=16)
            b.op("gpsimd", lambda eng: eng.collective_compute(
                "AllGather", OP.bypass, replica_groups=[list(range(NCORE))],
                ins=[v_loc.ap().opt()], outs=[v_all.ap().opt()]),
                ch="cc", inc=16)
        cc_done = b.cnt["cc"]

        kv_done = [0, 0]
        kv_val = {}

        def load_head(h):
            kk = h % 2
            kch = "ka" if kk == 0 else "kb"
            vch = "va" if kk == 0 else "vb"
            b.wait("sync", "cc", cc_done)
            b.wait("sync", "dg")
            b.wait("sync", "pe", kv_done[kk])
            kv = b.op("sync", lambda eng, h=h, kk=kk: eng.dma_start(
                out=kbuf[kk][:, :],
                in_=bass.AP(k_all, h * 131072,
                            [[1024, 128], [2097152, 8], [1, 1024]])),
                ch=kch, inc=16)
            for r in range(NCORE):
                vv = b.op("sync", lambda eng, h=h, r=r, kk=kk: eng.dma_start(
                    out=bass.AP(vbuf[kk], r * 1024,
                                [[8192, 128], [128, 8], [1, 128]]),
                    in_=bass.AP(v_all, r * 1024 * 2048 + h * 128,
                                [[2048, 128], [262144, 8], [1, 128]])),
                    ch=vch, inc=16)
            kv_val[h] = (kch, kv, vch, vv)

        load_head(0)
        load_head(1)
        SCALE = float(1.0 / math.sqrt(128.0))
        for h in range(16):
            kch, kv, vch, vv = kv_val.pop(h)
            b.wait("tensor", kch, kv)
            b.wait("tensor", vch, vv)
            for qh in range(2):
                self.bank_acquire(2)
                self.bank_acquire(3)
                for kt in range(64):
                    bs = kt % 2
                    self.bank_acquire(bs)
                    b.op("tensor", lambda eng, h=h, qh=qh, kt=kt, bs=bs:
                         eng.matmul(
                             self.pb[bs][:, :], kbuf[h % 2][:, kt * 128:(kt + 1) * 128],
                             QT[:, h * 1024 + qh * 512: h * 1024 + qh * 512 + 512],
                             start=True, stop=True),
                         ch="pe", inc=1)
                    b.wait("scalar", "pe")
                    b.wait("scalar", "pa" if kt % 2 == 0 else "px",
                           pbuf_free[kt % 2])
                    b.op("scalar", lambda eng, kt=kt, bs=bs: eng.activation(
                        out=self.pbuf[kt % 2][:, :], in_=self.pb[bs][:, :],
                        func=AF.Exp, bias=np.float32(-ATT0_SHIFT),
                        scale=np.float32(SCALE)))
                    self.bank_release(bs, "se")
                    b.wait("tensor", "se")
                    b.op("tensor", lambda eng, kt=kt, qh=qh:
                         eng.matmul(
                             self.pb[3][0:1, :], self.ones_col[:, :],
                             self.pbuf[kt % 2][:, :],
                             start=(kt == 0), stop=(kt == 63)))
                    b.op("tensor", lambda eng, h=h, kt=kt:
                         eng.matmul(
                             self.pb[2][:, :], vbuf[h % 2][:, kt * 128:(kt + 1) * 128],
                             self.pbuf[kt % 2][:, :],
                             start=(kt == 0), stop=(kt == 63)),
                         ch="pe", inc=1)
                b.wait("vector", "pe")
                b.op("vector", lambda eng: eng.reciprocal(
                    self.row1[0:1, :], self.pb[3][0:1, :]))
                self.bank_release(3, "ve")
                b.op("vector", lambda eng: eng.tensor_copy(
                    self.rowb[0:1, :], self.row1[0:1, :]))
                self.bank_acquire(4)
                b.wait("tensor", "ve")
                b.op("tensor", lambda eng: eng.matmul(
                    self.pb[4][:, :], self.ones_row[:, :], self.rowb[0:1, :],
                    start=True, stop=True), ch="pe", inc=1)
                b.wait("scalar", "pe")
                b.wait("scalar", "ve")
                b.op("scalar", lambda eng: eng.activation(
                    out=self.sg[:, :], in_=self.pb[2][:, :], func=AF.Copy))
                self.bank_release(2, "se")
                b.wait("vector", "se")
                b.op("vector", lambda eng, h=h, qh=qh: eng.tensor_mul(
                    OT[:, h * 1024 + qh * 512: h * 1024 + qh * 512 + 512],
                    self.sg[:, :], self.pb[4][:, :]))
                self.bank_release(4, "ve")
            kv_done[h % 2] = b.cnt["pe"]
            if h + 2 < 16:
                load_head(h + 2)

        b.wait("sync", "pe")
        for tt in range(NT):
            b.op("sync", lambda eng, tt=tt: eng.dma_start(
                out=self.x[:, tt * D:(tt + 1) * D],
                in_=xsp[tt * 128:(tt + 1) * 128, :]), ch="dy", inc=16)
        x_back = b.cnt["dy"]
        b.wait("tensor", "ve")
        for j in range(8):
            w = ws.get()
            for tt in range(NT):
                bk = 4 + tt % 2
                self.bank_acquire(bk)
                for c in range(ND):
                    b.op("tensor", lambda eng, c=c, tt=tt, w=w, bk=bk:
                         eng.matmul(
                             self.pb[bk][:, 0:256],
                             OT[:, c * 1024 + tt * 128: c * 1024 + tt * 128 + 128],
                             w[:, c * 256:(c + 1) * 256],
                             start=(c == 0), stop=(c == ND - 1)),
                         ch="pe", inc=(1 if c == ND - 1 else 0))
                b.wait("vector", "pe")
                b.wait("vector", "dy", x_back)
                b.op("vector", lambda eng, j=j, tt=tt, bk=bk:
                     eng.scalar_tensor_tensor(
                         self.x[:, tt * D + j * 256: tt * D + j * 256 + 256],
                         self.pb[bk][:, 0:256], 1.0,
                         self.x[:, tt * D + j * 256: tt * D + j * 256 + 256],
                         OP.mult, OP.add))
                self.bank_release(bk, "ve")
            ws.done()
        assert not ws.plan and not ws.dma_done


    def _top2(self, tt, src):
        b = self.b
        m1 = self.stat[:, 8:9]
        m2 = self.stat[:, 9:10]
        nm1 = self.stat[:, 10:11]
        e2 = self.stat[:, 11:12]
        den = self.stat[:, 12:13]
        w1 = self.stat[:, 13:14]
        w2 = self.stat[:, 14:15]
        b.op("vector", lambda eng, o=m1: eng.tensor_reduce(o, src, AX, OP.max))
        b.op("vector", lambda eng, m=m1: eng.tensor_scalar(
            self.r8a[:, :], src, m, None, OP.is_equal))
        b.op("vector", lambda eng: eng.scalar_tensor_tensor(
            self.r8b[:, :], self.r8a[:, :], -1e30, src, OP.mult, OP.add))
        b.op("vector", lambda eng, o=m2: eng.tensor_reduce(
            o, self.r8b[:, :], AX, OP.max))
        b.op("vector", lambda eng, m=m2: eng.tensor_scalar(
            self.r8b[:, :], self.r8b[:, :], m, None, OP.is_equal))
        b.op("vector", lambda eng, m=m1, o=nm1: eng.tensor_scalar_mul(
            o, m, -1.0))
        b.wait("scalar", "ve")
        b.op("scalar", lambda eng, m=m2, o=e2, bb=nm1: eng.activation(
            out=o, in_=m, func=AF.Exp, bias=bb))
        b.wait("vector", "se")
        b.op("vector", lambda eng, i=e2, o=den: eng.tensor_scalar(
            o, i, 1.0, None, OP.add))
        b.op("vector", lambda eng, i=den, o=w1: eng.reciprocal(o, i))
        b.op("vector", lambda eng, o=w2, a=e2, c=w1: eng.tensor_mul(o, a, c))
        b.op("vector", lambda eng, w=w1: eng.tensor_scalar(
            self.r8a[:, :], self.r8a[:, :], w, None, OP.mult))
        b.op("vector", lambda eng, tt=tt, w=w2: eng.scalar_tensor_tensor(
            self.cmb[:, tt * 8:(tt + 1) * 8], self.r8b[:, :], w,
            self.r8a[:, :], OP.mult, OP.add))

    def emit_ffn1(self, tag):
        b = self.b
        ws = self.ws
        NI = 28
        self.emit_ln(tag)

        confw_d = self.din("f1_confw", [128, 128])
        small_d = self.din("f1_small", [2, 8])
        bg_d = self.din("f1_bg", [3584, 4096])
        bu_d = self.din("f1_bu", [3584, 4096])
        bd_d = self.din("f1_bd", [512, 28672])
        gA_d = self.din("f1_gA", [128, 8192])
        uA_d = self.din("f1_uA", [128, 8192])
        gB_d = self.din("f1_gB", [64, 57344])
        uB_d = self.din("f1_uB", [64, 57344])
        dA_d = self.din("f1_dA", [128, 28672])
        dB_d = self.din("f1_dB", [64, 16384])
        hbar_d = self.dint("f1_hbar", [7168, 1024], BF16)
        xsp = self.dint("xspill", [T, D], F32)

        AX0 = self.A_X
        zg = b.carve("f1_zg", [128, 4096], BF16, AX0)
        zu = b.carve("f1_zu", [128, 4096], BF16, AX0 + 8192)
        ybuf = b.carve("f1_y", [128, 4096], F32, AX0 + 16384)
        ztmp = b.carve("f1_ztmp", [128, 4096], F32, AX0 + 16384)
        hb32 = b.carve("f1_hb32", [128, 2048], F32, AX0 + 32768)
        hb16 = [b.carve(f"f1_hb16{k}", [128, 2048], BF16,
                        AX0 + 40960 + k * 4096) for k in (0, 1)]
        he = b.carve("f1_he", [128, 2048], BF16, AX0 + 49152)
        gBb = b.carve("f1_gBb", [128, 2048], BF16, AX0 + 53248)
        uBb = b.carve("f1_uBb", [128, 2048], BF16, AX0 + 57344)
        dAb = b.carve("f1_dAb", [128, 1024], BF16, AX0 + 61440)
        bg_sb = b.carve("f1_bg_sb", [128, 2048], F32, self.A_FREE)
        bu_sb = b.carve("f1_bu_sb", [128, 2048], F32, self.A_FREE + 8192)
        ybb = b.carve("f1_ybb", [64, 8192], BF16, self.A_FREE)
        hbuf = [b.carve(f"f1_hbuf{k}", [128, 2048], BF16,
                        self.A_FREE + 16384 + k * 4096) for k in (0, 1)]

        def zsl(zt, e, th=None):
            p0 = 64 * (e % 2)
            c0 = (e // 2) * 1024
            if th is None:
                return zt[p0:p0 + 64, c0:c0 + 1024]
            return zt[p0:p0 + 64, c0 + th * 512:c0 + th * 512 + 512]

        b.wait("sync", "ve")
        b.op("sync", lambda eng: eng.dma_start(
            out=self.wstage[0][:, 0:128], in_=confw_d[:, :]), ch="dy", inc=16)
        b.op("sync", lambda eng: eng.dma_start(
            out=self.small[0:1, 0:8], in_=small_d[0:1, :]), ch="dy", inc=16)
        b.op("sync", lambda eng: eng.dma_start(
            out=self.small[0:1, 8:16], in_=small_d[1:2, :]), ch="dy", inc=16)
        b.wait("vector", "dy")
        self.ws.cast_done[0] = b.op("vector", lambda eng: eng.tensor_copy(
            self.confw_bf[:, :], self.wstage[0][:, 0:128]))
        self.bank_acquire(5)
        b.wait("tensor", "ve")
        b.wait("tensor", "dy")
        b.op("tensor", lambda eng: eng.matmul(
            self.pb[5][:, 0:8], self.onesf[:, :], self.small[0:1, 0:8],
            start=True, stop=True))
        b.op("tensor", lambda eng: eng.matmul(
            self.pb[5][:, 8:16], self.onesf[:, :], self.small[0:1, 8:16],
            start=True, stop=True), ch="pe", inc=1)
        b.wait("vector", "pe")
        b.op("vector", lambda eng: eng.tensor_copy(
            self.small[:, 0:16], self.pb[5][:, 0:16]))
        self.bank_release(5, "ve")
        cb_bc = self.small[:, 0:8]
        wl_bc = self.small[:, 8:16]

        b.wait("tensor", "ve")
        for tt in range(NT):
            bk = 4 + tt % 2
            self.bank_acquire(bk)
            for c in range(ND):
                b.op("tensor", lambda eng, tt=tt, c=c, bk=bk: eng.matmul(
                    self.pb[bk][:, 0:8],
                    self.hT_tile(c, tt * 128, 128),
                    self.confw_bf[:, c * 8:(c + 1) * 8],
                    start=(c == 0), stop=(c == ND - 1)),
                    ch="pe", inc=(1 if c == ND - 1 else 0))
            b.wait("vector", "pe")
            b.op("vector", lambda eng, bk=bk: eng.tensor_add(
                self.r8b[:, :], self.pb[bk][:, 0:8], cb_bc))
            self.bank_release(bk, "ve")
            b.wait("scalar", "ve")
            b.op("scalar", lambda eng: eng.activation(
                out=self.r8a[:, :], in_=self.r8b[:, :], func=AF.Sigmoid))
            b.wait("vector", "se")
            b.op("vector", lambda eng: eng.tensor_mul(
                self.r8b[:, :], self.r8a[:, :], wl_bc))
            self._top2(tt, self.r8b[:, :])
        b.op("vector", lambda eng: eng.tensor_copy(
            self.cmb_bf[:, :], self.cmb[:, :]))
        b.wait("tensor", "ve")
        for tt in range(NT):
            b.op("tensor", lambda eng, tt=tt: eng.transpose(
                self.pt[0:8, tt * 128:(tt + 1) * 128],
                self.cmb_bf[:, tt * 8:(tt + 1) * 8], self.ident[:, :]),
                ch="pe", inc=(1 if tt == NT - 1 else 0))
        b.wait("vector", "pe")
        b.op("vector", lambda eng: eng.tensor_copy(
            self.weT[:, :], self.pt[0:8, 0:1024]))

        def cbc(e, th, bk):
            self.bank_acquire(bk)
            b.wait("tensor", "ve")
            return b.op("tensor", lambda eng, e=e, th=th, bk=bk: eng.matmul(
                self.pb[bk][:, :], self.sel8[:, e * 128:(e + 1) * 128],
                self.weT[0:8, th * 512:(th + 1) * 512],
                start=True, stop=True), ch="pe", inc=1)

        b.wait("gpsimd", "ve")
        for tt in range(NT):
            b.op("gpsimd", lambda eng, tt=tt: eng.dma_start(
                out=xsp[tt * 128:(tt + 1) * 128, :],
                in_=self.x[:, tt * D:(tt + 1) * D]), ch="dg", inc=16)
        dg_spill = b.cnt["dg"]

        plan = []
        if FFN1_CUT < 2:
            ws.begin(plan)
        for gu in range(2 if FFN1_CUT >= 2 else 0):
            d = gA_d if gu == 0 else uA_d
            for ch in range(2):
                plan.append((d[:, ch * 4096:(ch + 1) * 4096], 4096))
        for ck in range(NI if FFN1_CUT >= 3 else 0):
            plan.append((bg_d[ck * 128:(ck + 1) * 128, :], 4096))
            plan.append((bu_d[ck * 128:(ck + 1) * 128, :], 4096))
            plan.append({"ap": bass.AP(gB_d, ck * 256,
                                       [[57344, 64], [7168, 8], [1, 256]]),
                         "cols": 2048, "rows": 64, "dst": gBb})
            plan.append({"ap": bass.AP(uB_d, ck * 256,
                                       [[57344, 64], [7168, 8], [1, 256]]),
                         "cols": 2048, "rows": 64, "dst": uBb})
            plan.append({"ap": bass.AP(dA_d, ck * 128,
                                       [[28672, 128], [3584, 8],
                                        [64, 2], [1, 64]]),
                         "cols": 1024, "rows": 128, "dst": dAb})
        if FFN1_CUT >= 4:
            plan += [{"ap": bass.AP(dB_d, fb * 512,
                                    [[16384, 64], [2048, 8], [1, 512]]),
                      "cols": 4096, "rows": 64} for fb in range(4)]
        for g in range(28 if FFN1_CUT >= 5 else 0):
            for fb in range(4):
                plan.append((bd_d[fb * 128:(fb + 1) * 128,
                                  g * 1024:(g + 1) * 1024], 1024))
        if FFN1_CUT >= 2:
            ws.begin(plan)

        if FFN1_CUT < 2:
            b.wait("sync", "ve")
            b.wait("sync", "pe")
            b.wait("sync", "se")
            b.wait("sync", "ha")
            b.wait("sync", "hb")
            for tt in range(NT):
                b.op("sync", lambda eng, tt=tt: eng.dma_start(
                    out=self.x[:, tt * D:(tt + 1) * D],
                    in_=xsp[tt * 128:(tt + 1) * 128, :]), ch="dy", inc=16)
            b.wait("vector", "dy")
            b.wait("gpsimd", "dy")
            return

        b.wait("vector", "dg", dg_spill)
        for gu in range(2):
            zdst = zg if gu == 0 else zu
            for ch in range(2):
                w = ws.get()
                for e in range(8):
                    for th in range(2):
                        bk = (e * 2 + th) % 2
                        self.bank_acquire(bk)
                        for c8 in range(8):
                            c = ch * 8 + c8
                            b.op("tensor",
                                 lambda eng, c=c, c8=c8, e=e, th=th, w=w, bk=bk:
                                 eng.matmul(
                                     self.pb[bk][0:64, :],
                                     w[:, c8 * 512 + e * 64:
                                       c8 * 512 + e * 64 + 64],
                                     self.hT_tile(c, th * 512, 512),
                                     start=(c8 == 0), stop=(c8 == 7)),
                                 ch="pe", inc=(1 if c8 == 7 else 0))
                        b.wait("vector", "pe")
                        if ch == 0:
                            b.op("vector", lambda eng, e=e, th=th, bk=bk:
                                 eng.tensor_scalar(
                                     ztmp[0:64, e * 1024 + th * 512 - (e // 2)
                                          * 2048 + (e // 2) * 2048:
                                          e * 1024 + th * 512 + 512]
                                     if False else
                                     ztmp[0:64, (e % 4) * 1024 + th * 512:
                                          (e % 4) * 1024 + th * 512 + 512]
                                     if e < 4 else
                                     ztmp[64:128, (e % 4) * 1024 + th * 512:
                                          (e % 4) * 1024 + th * 512 + 512],
                                     self.pb[bk][0:64, :], 0.25, None,
                                     OP.mult))
                        else:
                            b.wait("vector", "dg", dg_spill)
                            zt = (ztmp[0:64, (e % 4) * 1024 + th * 512:
                                       (e % 4) * 1024 + th * 512 + 512]
                                  if e < 4 else
                                  ztmp[64:128, (e % 4) * 1024 + th * 512:
                                       (e % 4) * 1024 + th * 512 + 512])
                            b.op("vector", lambda eng, e=e, th=th, bk=bk,
                                 zt=zt, zdst=zdst:
                                 eng.scalar_tensor_tensor(
                                     zsl(zdst, e, th),
                                     self.pb[bk][0:64, :], 0.25, zt,
                                     OP.mult, OP.add))
                        self.bank_release(bk, "ve")
                ws.done()

        if FFN1_CUT < 3:
            b.wait("sync", "ve")
            b.wait("sync", "pe")
            b.wait("sync", "se")
            b.wait("sync", "ha")
            b.wait("sync", "hb")
            for tt in range(NT):
                b.op("sync", lambda eng, tt=tt: eng.dma_start(
                    out=self.x[:, tt * D:(tt + 1) * D],
                    in_=xsp[tt * 128:(tt + 1) * 128, :]), ch="dy", inc=16)
            b.wait("vector", "dy")
            b.wait("gpsimd", "dy")
            return

        hdma = [0, 0]
        lora_war = {"g": 0, "u": 0, "d": 0}
        for ck in range(NI):
            for gu in range(2):
                dst = bg_sb if gu == 0 else bu_sb
                w = ws.get()
                for itl in range(2):
                    for th in range(2):
                        bk = (itl * 2 + th) % 2
                        self.bank_acquire(bk)
                        for c in range(ND):
                            b.op("tensor",
                                 lambda eng, c=c, itl=itl, th=th, w=w, bk=bk:
                                 eng.matmul(
                                     self.pb[bk][:, :],
                                     w[:, c * 256 + itl * 128:
                                       c * 256 + itl * 128 + 128],
                                     self.hT_tile(c, th * 512, 512),
                                     start=(c == 0), stop=(c == ND - 1)),
                                 ch="pe", inc=(1 if c == ND - 1 else 0))
                        b.wait("vector", "pe")
                        if gu == 0 and itl == 0 and th == 0:
                            b.wait("vector", "dg", dg_spill)
                        off = itl * 1024 + th * 512
                        b.op("vector", lambda eng, off=off, bk=bk, dst=dst:
                             eng.tensor_copy(
                                 dst[:, off:off + 512], self.pb[bk][:, :]))
                        self.bank_release(bk, "ve")
                ws.done()
            gBw = ws.get(war=("pe", lora_war["g"]))
            ws.done()
            b.op("vector", lambda eng: eng.tensor_copy(
                gBb[64:128, :], gBb[0:64, :]))
            uBw = ws.get(war=("pe", lora_war["u"]))
            ws.done()
            b.op("vector", lambda eng: eng.tensor_copy(
                uBb[64:128, :], uBb[0:64, :]))
            dAw = ws.get(war=("pe", lora_war["d"]))
            ws.done()
            b.wait("tensor", "ve")
            for e in range(8):
                c0 = cbc(e, 0, 2)
                c1 = cbc(e, 1, 3)
                for itl in range(2):
                    for th in range(2):
                        bkg, bku = 0, 1
                        self.bank_acquire(bkg)
                        p0 = 64 * (e % 2)
                        b.op("tensor", lambda eng, e=e, itl=itl, th=th, p0=p0:
                             eng.matmul(
                                 self.pb[0][:, :],
                                 gBb[p0:p0 + 64, e * 256 + itl * 128:
                                     e * 256 + itl * 128 + 128],
                                 zsl(zg, e, th),
                                 start=True, stop=True), ch="pe", inc=1)
                        self.bank_acquire(bku)
                        b.op("tensor", lambda eng, e=e, itl=itl, th=th, p0=p0:
                             eng.matmul(
                                 self.pb[1][:, :],
                                 uBb[p0:p0 + 64, e * 256 + itl * 128:
                                     e * 256 + itl * 128 + 128],
                                 zsl(zu, e, th),
                                 start=True, stop=True), ch="pe", inc=1)
                        b.wait("vector", "pe")
                        off = itl * 1024 + th * 512
                        b.op("vector", lambda eng, off=off: eng.tensor_add(
                            self.t1[:, :], self.pb[0][:, :],
                            bg_sb[:, off:off + 512]))
                        self.bank_release(bkg, "ve")
                        b.wait("scalar", "ve")
                        b.op("scalar", lambda eng: eng.activation(
                            out=self.t2[:, :], in_=self.t1[:, :],
                            func=AF.Silu))
                        b.wait("vector", "se")
                        b.op("vector", lambda eng, off=off: eng.tensor_add(
                            self.t1[:, :], self.pb[1][:, :],
                            bu_sb[:, off:off + 512]))
                        self.bank_release(bku, "ve")
                        b.op("vector", lambda eng, off=off: eng.tensor_mul(
                            he[:, off:off + 512], self.t2[:, :],
                            self.t1[:, :]))
                        cb = self.pb[2] if th == 0 else self.pb[3]
                        b.op("vector", lambda eng, off=off, cb=cb:
                             eng.tensor_mul(
                                 self.t1[:, :], he[:, off:off + 512],
                                 cb[:, :]))
                        if e == 0:
                            b.op("vector", lambda eng, off=off:
                                 eng.tensor_copy(
                                     hb32[:, off:off + 512], self.t1[:, :]))
                        else:
                            b.op("vector", lambda eng, off=off:
                                 eng.tensor_add(
                                     hb32[:, off:off + 512],
                                     hb32[:, off:off + 512], self.t1[:, :]))
                b.wait("tensor", "ve")
                for th in range(2):
                    self.bank_acquire(4)
                    for itl in range(2):
                        b.op("tensor", lambda eng, e=e, itl=itl, th=th:
                             eng.matmul(
                                 self.pb[4][0:64, :],
                                 dAb[:, (e * 2 + itl) * 64:
                                     (e * 2 + itl) * 64 + 64],
                                 he[:, itl * 1024 + th * 512:
                                    itl * 1024 + th * 512 + 512],
                                 start=(itl == 0), stop=(itl == 1)),
                             ch="pe", inc=(1 if itl == 1 else 0))
                    b.wait("vector", "pe")
                    if ck == 0:
                        b.op("vector", lambda eng, e=e, th=th:
                             eng.tensor_copy(
                                 zsl(ybuf, e, th), self.pb[4][0:64, :]))
                    else:
                        b.op("vector", lambda eng, e=e, th=th:
                             eng.tensor_add(
                                 zsl(ybuf, e, th), zsl(ybuf, e, th),
                                 self.pb[4][0:64, :]))
                    self.bank_release(4, "ve")
                self.bank_release(2, "ve")
                self.bank_release(3, "ve")
            lora_war = {"g": b.cnt["pe"], "u": b.cnt["pe"], "d": b.cnt["pe"]}
            hch = "ha" if ck % 2 == 0 else "hb"
            b.wait("vector", hch, hdma[ck % 2])
            b.op("vector", lambda eng, kk=ck % 2: eng.tensor_copy(
                hb16[kk][:, :], hb32[:, :]))
            b.wait("gpsimd", "ve")
            hdma[ck % 2] = b.op("gpsimd", lambda eng, ck=ck: eng.dma_start(
                out=bass.AP(hbar_d, ck * 256 * 1024,
                            [[1024, 128], [131072, 2], [1, 1024]]),
                in_=hb16[ck % 2][:, :]), ch=hch, inc=16)

        if FFN1_CUT < 4:
            b.wait("sync", "ve")
            b.wait("sync", "pe")
            b.wait("sync", "se")
            b.wait("sync", "ha")
            b.wait("sync", "hb")
            for tt in range(NT):
                b.op("sync", lambda eng, tt=tt: eng.dma_start(
                    out=self.x[:, tt * D:(tt + 1) * D],
                    in_=xsp[tt * 128:(tt + 1) * 128, :]), ch="dy", inc=16)
            b.wait("vector", "dy")
            b.wait("gpsimd", "dy")
            return

        for e in range(8):
            for th in range(2):
                c0 = cbc(e, th, 2 + th % 2)
                b.wait("vector", "pe")
                cb = self.pb[2 + th % 2]
                b.op("vector", lambda eng, e=e, th=th, cb=cb:
                     eng.scalar_tensor_tensor(
                         ybb[0:64, e * 1024 + th * 512:
                             e * 1024 + th * 512 + 512],
                         zsl(ybuf, e, th), 0.25, cb[0:64, :],
                         OP.mult, OP.mult))
                self.bank_release(2 + th % 2, "ve")

        b.wait("sync", "ve")
        b.wait("sync", "pe")
        b.wait("sync", "ha")
        b.wait("sync", "hb")
        for tt in range(NT):
            b.op("sync", lambda eng, tt=tt: eng.dma_start(
                out=self.x[:, tt * D:(tt + 1) * D],
                in_=xsp[tt * 128:(tt + 1) * 128, :]), ch="dy", inc=16)
        x_back = b.cnt["dy"]
        b.wait("vector", "dy", x_back)

        for fb in range(4):
            w = ws.get()
            b.wait("tensor", "ve")
            for tt in range(NT):
                bk = 4 + tt % 2
                self.bank_acquire(bk)
                for e in range(8):
                    b.op("tensor", lambda eng, e=e, tt=tt, fb=fb, w=w, bk=bk:
                         eng.matmul(
                             self.pb[bk][:, :],
                             ybb[0:64, e * 1024 + tt * 128:
                                 e * 1024 + tt * 128 + 128],
                             w[0:64, e * 512:(e + 1) * 512],
                             start=(e == 0), stop=(e == 7)),
                         ch="pe", inc=(1 if e == 7 else 0))
                b.wait("vector", "pe")
                b.op("vector", lambda eng, tt=tt, fb=fb, bk=bk:
                     eng.scalar_tensor_tensor(
                         self.x[:, tt * D + fb * 512:
                                tt * D + fb * 512 + 512],
                         self.pb[bk][:, :], 0.5,
                         self.x[:, tt * D + fb * 512:
                                tt * D + fb * 512 + 512],
                         OP.mult, OP.add))
                self.bank_release(bk, "ve")
            ws.done()

        if FFN1_CUT < 5:
            return
        hload = [0, 0]
        hfree = [0, 0]
        for g in range(28):
            kk = g % 2
            hc = "ka" if kk == 0 else "kb"
            b.wait("sync", hc, hload[kk])
            b.wait("sync", "pe", hfree[kk])
            b.wait("sync", "ve")
            b.wait("sync", "ha")
            b.wait("sync", "hb")
            hload[kk] = b.op("sync", lambda eng, g=g, kk=kk: eng.dma_start(
                out=hbuf[kk][:, :],
                in_=bass.AP(hbar_d, g * 256 * 1024,
                            [[1024, 128], [131072, 2], [1, 1024]])),
                ch=hc, inc=16)
            for fb in range(4):
                w = ws.get()
                b.wait("tensor", hc, hload[kk])
                for tt in range(NT):
                    bk = 4 + tt % 2
                    self.bank_acquire(bk)
                    for ic in range(2):
                        b.op("tensor",
                             lambda eng, ic=ic, tt=tt, w=w, bk=bk, kk=kk:
                             eng.matmul(
                                 self.pb[bk][:, :],
                                 hbuf[kk][:, ic * 1024 + tt * 128:
                                          ic * 1024 + tt * 128 + 128],
                                 w[:, ic * 512:(ic + 1) * 512],
                                 start=(ic == 0), stop=(ic == 1)),
                             ch="pe", inc=(1 if ic == 1 else 0))
                    b.wait("vector", "pe")
                    b.op("vector", lambda eng, tt=tt, fb=fb, bk=bk:
                         eng.scalar_tensor_tensor(
                             self.x[:, tt * D + fb * 512:
                                    tt * D + fb * 512 + 512],
                             self.pb[bk][:, :], 0.5,
                             self.x[:, tt * D + fb * 512:
                                    tt * D + fb * 512 + 512],
                             OP.mult, OP.add))
                    self.bank_release(bk, "ve")
                ws.done()
            hfree[kk] = b.cnt["pe"]
        assert not ws.plan and not ws.dma_done


    def emit_att1(self, tag, mode="fused"):
        b = self.b
        ws = self.ws
        pre = mode in ("fused", "pre")
        post = mode in ("fused", "post")
        if pre:
            self.emit_ln(tag)

        if pre:
            wq_d = self.din("a1_wq", [512, 4096])
            wk_d = self.din("a1_wk", [512, 4096])
            wv_d = self.din("a1_wv", [1024, 4096])
            wg_d = self.din("a1_wg", [2048, 2048])
        if post:
            wo_d = self.din("a1_wo", [1024, 4096])
        ls_d = self.din("a1_ls", [2, 16])
        if mode == "fused":
            k_loc = self.dint("a1k_loc", [1024, 1024], BF16)
            k_all = self.dint("a1k_all", [8192, 1024], BF16)
            v_loc = self.dint("a1v_loc", [1024, 2048], BF16)
            v_all = self.dint("a1v_all", [8192, 2048], BF16)
        elif mode == "pre":
            k_loc = self.dout("a1k_loc", [1024, 1024], BF16)
            v_loc = self.dout("a1v_loc", [1024, 2048], BF16)
            qt_io = self.dout("a1q_dump", [128, 8192], BF16)
            gt_io = self.dout("a1gt_dump", [128, 16384], BF16)
        else:
            k_all = self.din("a1k_all", [8192, 1024], BF16)
            v_all = self.din("a1v_all", [8192, 2048], BF16)
            qt_io = self.din("a1q_dump", [128, 8192], BF16)
            gt_io = self.din("a1gt_dump", [128, 16384], BF16)
        xsp = self.dint("xspill", [T, D], F32)

        kbuf = b.carve("a1_kb", [128, 8192], BF16, self.A_X)
        QT1 = b.carve("a1_QT", [128, 8192], BF16, self.A_X + 16384)
        gt = b.carve("a1_gt", [128, 16384], BF16, self.A_X + 32768)
        vbuf = [b.carve(f"a1_vb{k}", [128, 8192], BF16,
                        self.A_FREE + k * 16384) for k in (0, 1)]
        OT1 = b.carve("a1_OT", [128, 16384], BF16, self.A_HT)

        b.wait("sync", "ve")
        b.op("sync", lambda eng: eng.dma_start(
            out=self.small[0:1, 0:16], in_=ls_d[0:1, :]), ch="dy", inc=16)
        b.op("sync", lambda eng: eng.dma_start(
            out=self.small[0:1, 16:32], in_=ls_d[1:2, :]), ch="dy", inc=16)
        b.wait("scalar", "dy")
        b.wait("scalar", "ve")
        b.op("scalar", lambda eng: eng.activation(
            out=self.small[0:1, 32:48], in_=self.small[0:1, 0:16], func=AF.Exp))
        b.wait("vector", "se")
        b.op("vector", lambda eng: eng.tensor_scalar(
            self.small[0:1, 32:48], self.small[0:1, 32:48], 1.0, None, OP.add))
        b.op("vector", lambda eng: eng.reciprocal(
            self.small[0:1, 32:48], self.small[0:1, 32:48]))
        b.wait("scalar", "ve")
        b.op("scalar", lambda eng: eng.activation(
            out=self.small[0:1, 48:64], in_=self.small[0:1, 16:32],
            func=AF.Exp))
        self.bank_acquire(5)
        b.wait("tensor", "se")
        b.wait("tensor", "ve")
        b.op("tensor", lambda eng: eng.matmul(
            self.pb[5][:, 0:16], self.onesf[:, :], self.small[0:1, 32:48],
            start=True, stop=True))
        b.op("tensor", lambda eng: eng.matmul(
            self.pb[5][:, 16:32], self.onesf[:, :], self.small[0:1, 48:64],
            start=True, stop=True), ch="pe", inc=1)
        b.wait("vector", "pe")
        b.op("vector", lambda eng: eng.tensor_copy(
            self.small[:, 0:32], self.pb[5][:, 0:32]))
        b.op("vector", lambda eng: eng.tensor_scalar(
            self.small[:, 32:48], self.small[:, 0:16], -1.0, 1.0,
            OP.mult, OP.add))
        self.bank_release(5, "ve")
        rbc = self.small[:, 0:16]
        hwsb = self.small[:, 16:32]
        r1m = self.small[:, 32:48]

        b.wait("gpsimd", "ve")
        for tt in range(NT):
            b.op("gpsimd", lambda eng, tt=tt: eng.dma_start(
                out=xsp[tt * 128:(tt + 1) * 128, :],
                in_=self.x[:, tt * D:(tt + 1) * D]), ch="dg", inc=16)
        dg_spill = b.cnt["dg"]

        plan = []
        if pre:
            plan += [(wq_d[j * 128:(j + 1) * 128, :], 4096) for j in range(4)]
            plan += [(wk_d[j * 128:(j + 1) * 128, :], 4096) for j in range(4)]
            plan += [(wv_d[j * 128:(j + 1) * 128, :], 4096) for j in range(8)]
            plan += [(wg_d[h * 128:(h + 1) * 128, :], 2048) for h in range(16)]
        if post:
            plan += [(wo_d[j * 128:(j + 1) * 128, :], 4096) for j in range(8)]
        ws.begin(plan)
        if mode == "post":
            b.wait("sync", "ve")
            b.wait("sync", "pe")
            b.wait("sync", "dg")
            b.op("sync", lambda eng: eng.dma_start(
                out=QT1[:, :], in_=qt_io[:, :]), ch="dy", inc=16)
            b.op("sync", lambda eng: eng.dma_start(
                out=gt[:, :], in_=gt_io[:, :]), ch="dy", inc=16)
            b.wait("tensor", "dy")
            b.wait("vector", "dy")

        b.wait("vector", "dg", dg_spill)
        pbuf_free = [0, 0]

        def l2tile(bk, nh, hd, out_bf):
            b.wait("scalar", "pe")
            b.wait("scalar", "ve")
            b.wait("vector", "pe")
            b.op("scalar", lambda eng, bk=bk, n=nh * hd: eng.activation(
                out=self.t1[:, 0:n], in_=self.pb[bk][:, 0:n], func=AF.Square))
            b.wait("vector", "se")
            b.op("vector", lambda eng, nh=nh, hd=hd: eng.tensor_reduce(
                self.stat[:, 0:nh],
                bass.AP(self.t1, 0, [[512, 128], [hd, nh], [1, hd]]),
                AX, OP.add))
            b.wait("scalar", "ve")
            b.op("scalar", lambda eng, nh=nh: eng.activation(
                out=self.stat[:, 4:4 + nh], in_=self.stat[:, 0:nh],
                func=AF.Sqrt))
            b.wait("vector", "se")
            b.op("vector", lambda eng, nh=nh: eng.tensor_scalar_max(
                self.stat[:, 4:4 + nh], self.stat[:, 4:4 + nh], 1e-12))
            b.op("vector", lambda eng, nh=nh: eng.reciprocal(
                self.stat[:, 8:8 + nh], self.stat[:, 4:4 + nh]))
            for hh in range(nh):
                b.op("vector", lambda eng, hh=hh, hd=hd, bk=bk, out_bf=out_bf:
                     eng.tensor_scalar(
                         out_bf[:, hh * hd:(hh + 1) * hd],
                         self.pb[bk][:, hh * hd:(hh + 1) * hd],
                         self.stat[:, 8 + hh:9 + hh], None, OP.mult))

        for is_k in ((0, 1) if pre else ()):
            for j in range(4):
                w = ws.get()
                for tt in range(NT):
                    bk = tt % 2
                    self.bank_acquire(bk)
                    for c in range(ND):
                        b.op("tensor", lambda eng, c=c, tt=tt, w=w, bk=bk:
                             eng.matmul(
                                 self.pb[bk][:, 0:256],
                                 self.hT_tile(c, tt * 128, 128),
                                 w[:, c * 256:(c + 1) * 256],
                                 start=(c == 0), stop=(c == ND - 1)),
                             ch="pe", inc=(1 if c == ND - 1 else 0))
                    kk = tt % 2
                    b.wait("vector", "pa" if kk == 0 else "px", pbuf_free[kk])
                    l2tile(bk, 4, 64, self.pbuf[kk][:, 0:256])
                    self.bank_release(bk, "ve")
                    b.wait("tensor", "ve")
                    for hh in range(4):
                        b.op("tensor", lambda eng, hh=hh, kk=kk: eng.transpose(
                            self.pt[0:64, hh * 128:(hh + 1) * 128],
                            self.pbuf[kk][:, hh * 64:(hh + 1) * 64],
                            self.ident[:, :]),
                            ch="pe", inc=(1 if hh == 3 else 0))
                    b.wait("vector", "pe")
                    if not is_k:
                        for hh in range(4):
                            h = j * 4 + hh
                            p0 = 64 * (h % 2)
                            b.op("vector", lambda eng, hh=hh, h=h, p0=p0, tt=tt:
                                 eng.tensor_copy(
                                     QT1[p0:p0 + 64,
                                         (h // 2) * 1024 + tt * 128:
                                         (h // 2) * 1024 + tt * 128 + 128],
                                     self.pt[0:64, hh * 128:(hh + 1) * 128]))
                    else:
                        b.op("vector", lambda eng, kk=kk: eng.tensor_copy(
                            self.pbuf[1 - kk][0:64, 0:512],
                            self.pt[0:64, 0:512]))
                        b.wait("gpsimd", "ve")
                        pbuf_free[1 - kk] = b.op(
                            "gpsimd", lambda eng, j=j, tt=tt, kk=kk:
                            eng.dma_start(
                                out=bass.AP(k_loc, j * 4 * 64 * 1024 + tt * 128,
                                            [[1024, 64], [65536, 4], [1, 128]]),
                                in_=bass.AP(self.pbuf[1 - kk], 0,
                                            [[512, 64], [128, 4], [1, 128]])),
                            ch="pa" if (1 - kk) == 0 else "px", inc=16)
                ws.done()

        for j in (range(8) if pre else ()):
            w = ws.get()
            for tt in range(NT):
                bk = tt % 2
                self.bank_acquire(bk)
                for c in range(ND):
                    b.op("tensor", lambda eng, c=c, tt=tt, w=w, bk=bk:
                         eng.matmul(
                             self.pb[bk][:, 0:256],
                             self.hT_tile(c, tt * 128, 128),
                             w[:, c * 256:(c + 1) * 256],
                             start=(c == 0), stop=(c == ND - 1)),
                         ch="pe", inc=(1 if c == ND - 1 else 0))
                kk = tt % 2
                b.wait("vector", "pa" if kk == 0 else "px", pbuf_free[kk])
                l2tile(bk, 2, 128, self.pbuf[kk][:, 0:256])
                self.bank_release(bk, "ve")
                b.wait("gpsimd", "ve")
                pbuf_free[kk] = b.op(
                    "gpsimd", lambda eng, j=j, tt=tt, kk=kk: eng.dma_start(
                        out=v_loc[tt * 128:(tt + 1) * 128,
                                  j * 256:(j + 1) * 256],
                        in_=self.pbuf[kk][:, 0:256]),
                    ch="pa" if kk == 0 else "px", inc=16)
            ws.done()

        for h in (range(16) if pre else ()):
            w = ws.get()
            for th in range(2):
                bk = (h * 2 + th) % 2
                self.bank_acquire(bk)
                for c in range(ND):
                    b.op("tensor", lambda eng, c=c, th=th, w=w, bk=bk:
                         eng.matmul(
                             self.pb[bk][:, :], w[:, c * 128:(c + 1) * 128],
                             self.hT_tile(c, th * 512, 512),
                             start=(c == 0), stop=(c == ND - 1)),
                         ch="pe", inc=(1 if c == ND - 1 else 0))
                b.wait("scalar", "pe")
                b.wait("scalar", "ve")
                b.op("scalar", lambda eng, bk=bk: eng.activation(
                    out=self.t1[:, :], in_=self.pb[bk][:, :], func=AF.Silu))
                self.bank_release(bk, "se")
                b.op("scalar", lambda eng, h=h, th=th: eng.activation(
                    out=gt[:, h * 1024 + th * 512: h * 1024 + th * 512 + 512],
                    in_=self.t1[:, :], func=AF.Tanh), ch="se", inc=1)
            ws.done()

        if mode == "pre":
            b.wait("gpsimd", "ve")
            b.wait("gpsimd", "se")
            b.op("gpsimd", lambda eng: eng.dma_start(
                out=qt_io[:, :], in_=QT1[:, :]), ch="dg", inc=16)
            b.op("gpsimd", lambda eng: eng.dma_start(
                out=gt_io[:, :], in_=gt[:, :]), ch="dg", inc=16)
            b.wait("sync", "dg")
            b.wait("sync", "pa")
            b.wait("sync", "px")
            for tt in range(NT):
                b.op("sync", lambda eng, tt=tt: eng.dma_start(
                    out=self.x[:, tt * D:(tt + 1) * D],
                    in_=xsp[tt * 128:(tt + 1) * 128, :]), ch="dy", inc=16)
            b.wait("gpsimd", "dy")
            return
        if mode == "fused":
            b.wait("gpsimd", "dg")
            b.wait("gpsimd", "pa")
            b.wait("gpsimd", "px")
            b.op("gpsimd", lambda eng: eng.collective_compute(
                "AllGather", OP.bypass, replica_groups=[list(range(NCORE))],
                ins=[k_loc.ap().opt()], outs=[k_all.ap().opt()]),
                ch="cc", inc=16)
            b.op("gpsimd", lambda eng: eng.collective_compute(
                "AllGather", OP.bypass, replica_groups=[list(range(NCORE))],
                ins=[v_loc.ap().opt()], outs=[v_all.ap().opt()]),
                ch="cc", inc=16)
        cc_done = b.cnt["cc"]

        kv_done = [0, 0]
        kv_val = {}

        def load_head(h):
            p0 = 64 * (h % 2)
            kk = h % 2
            kch = "ka" if kk == 0 else "kb"
            vch = "va" if kk == 0 else "vb"
            b.wait("sync", "cc", cc_done)
            b.wait("sync", "dg")
            b.wait("sync", "pe", kv_done[kk])
            kv = b.op("sync", lambda eng, h=h, p0=p0: eng.dma_start(
                out=kbuf[p0:p0 + 64, :],
                in_=bass.AP(k_all, h * 65536,
                            [[1024, 64], [1048576, 8], [1, 1024]])),
                ch=kch, inc=16)
            for r in range(NCORE):
                vv = b.op("sync", lambda eng, h=h, r=r, kk=kk: eng.dma_start(
                    out=bass.AP(vbuf[kk], r * 1024,
                                [[8192, 128], [128, 8], [1, 128]]),
                    in_=bass.AP(v_all, r * 1024 * 2048 + h * 128,
                                [[2048, 128], [262144, 8], [1, 128]])),
                    ch=vch, inc=16)
            kv_val[h] = (kch, kv, vch, vv)

        load_head(0)
        load_head(1)
        for h in range(16):
            p0 = 64 * (h % 2)
            kch, kv, vch, vv = kv_val.pop(h)
            b.wait("tensor", kch, kv)
            b.wait("tensor", vch, vv)
            for qh in range(2):
                self.bank_acquire(2)
                for kt in range(64):
                    bs = kt % 2
                    self.bank_acquire(bs)
                    b.op("tensor", lambda eng, h=h, qh=qh, kt=kt, bs=bs, p0=p0:
                         eng.matmul(
                             self.pb[bs][:, :],
                             kbuf[p0:p0 + 64, kt * 128:(kt + 1) * 128],
                             QT1[p0:p0 + 64,
                                 (h // 2) * 1024 + qh * 512:
                                 (h // 2) * 1024 + qh * 512 + 512],
                             start=True, stop=True),
                         ch="pe", inc=1)
                    b.wait("scalar", "pe")
                    b.wait("scalar", "ve")
                    b.op("scalar", lambda eng, h=h, bs=bs: eng.activation(
                        out=self.t1[:, :], in_=self.pb[bs][:, :],
                        func=AF.Relu, scale=rbc[:, h:h + 1],
                        bias=r1m[:, h:h + 1]))
                    self.bank_release(bs, "se")
                    b.wait("vector", "se")
                    b.op("vector", lambda eng, kt=kt: eng.tensor_mul(
                        self.pbuf[kt % 2][:, :], self.t1[:, :], self.t1[:, :]))
                    b.wait("tensor", "ve")
                    b.op("tensor", lambda eng, h=h, kt=kt, kk=h % 2:
                         eng.matmul(
                             self.pb[2][:, :],
                             vbuf[kk][:, kt * 128:(kt + 1) * 128],
                             self.pbuf[kt % 2][:, :],
                             start=(kt == 0), stop=(kt == 63)),
                         ch="pe", inc=1)
                b.wait("scalar", "pe")
                b.wait("scalar", "ve")
                b.op("scalar", lambda eng: eng.activation(
                    out=self.sg[:, :], in_=self.pb[2][:, :], func=AF.Copy))
                b.wait("vector", "se")
                b.op("vector", lambda eng: eng.tensor_mul(
                    self.t1[:, :], self.sg[:, :], self.sg[:, :]))
                b.op("vector", lambda eng, kk=0: eng.tensor_copy(
                    self.pbuf[0][:, :], self.t1[:, :]))
                self.bank_acquire(3)
                b.wait("tensor", "ve")
                b.op("tensor", lambda eng: eng.matmul(
                    self.pb[3][0:1, :], self.ones_col[:, :],
                    self.pbuf[0][:, :], start=True, stop=True),
                    ch="pe", inc=1)
                b.wait("scalar", "pe")
                b.op("scalar", lambda eng: eng.activation(
                    out=self.row1[0:1, :], in_=self.pb[3][0:1, :],
                    func=AF.Sqrt))
                self.bank_release(3, "se")
                b.wait("vector", "se")
                b.op("vector", lambda eng: eng.tensor_scalar_max(
                    self.row2[0:1, :], self.row1[0:1, :], 1e-12))
                b.op("vector", lambda eng: eng.reciprocal(
                    self.row2[0:1, :], self.row2[0:1, :]))
                b.wait("scalar", "ve")
                b.op("scalar", lambda eng: eng.activation(
                    out=self.row1[0:1, :], in_=self.row1[0:1, :],
                    func=AF.Tanh), ch="se", inc=1)
                b.wait("vector", "se")
                b.op("vector", lambda eng: eng.tensor_mul(
                    self.row1[0:1, :], self.row1[0:1, :], self.row2[0:1, :]))
                b.op("vector", lambda eng, h=h: eng.tensor_scalar(
                    self.row1[0:1, :], self.row1[0:1, :],
                    hwsb[0:1, h:h + 1], None, OP.mult))
                b.op("vector", lambda eng: eng.tensor_copy(
                    self.rowb[0:1, :], self.row1[0:1, :]))
                self.bank_acquire(4)
                b.wait("tensor", "ve")
                b.op("tensor", lambda eng: eng.matmul(
                    self.pb[4][:, :], self.ones_row[:, :], self.rowb[0:1, :],
                    start=True, stop=True), ch="pe", inc=1)
                b.wait("vector", "pe")
                b.op("vector", lambda eng: eng.tensor_mul(
                    self.t1[:, :], self.sg[:, :], self.pb[4][:, :]))
                self.bank_release(4, "ve")
                self.bank_release(2, "ve")
                b.op("vector", lambda eng, h=h, qh=qh: eng.tensor_mul(
                    OT1[:, h * 1024 + qh * 512: h * 1024 + qh * 512 + 512],
                    self.t1[:, :],
                    gt[:, h * 1024 + qh * 512: h * 1024 + qh * 512 + 512]))
            kv_done[h % 2] = b.cnt["pe"]
            if h + 2 < 16:
                load_head(h + 2)

        b.wait("sync", "pe")
        b.wait("sync", "ve")
        for tt in range(NT):
            b.op("sync", lambda eng, tt=tt: eng.dma_start(
                out=self.x[:, tt * D:(tt + 1) * D],
                in_=xsp[tt * 128:(tt + 1) * 128, :]), ch="dy", inc=16)
        x_back = b.cnt["dy"]
        b.wait("tensor", "ve")
        for j in range(8):
            w = ws.get()
            for tt in range(NT):
                bk = 4 + tt % 2
                self.bank_acquire(bk)
                for c in range(ND):
                    b.op("tensor", lambda eng, c=c, tt=tt, w=w, bk=bk:
                         eng.matmul(
                             self.pb[bk][:, 0:256],
                             OT1[:, c * 1024 + tt * 128:
                                 c * 1024 + tt * 128 + 128],
                             w[:, c * 256:(c + 1) * 256],
                             start=(c == 0), stop=(c == ND - 1)),
                         ch="pe", inc=(1 if c == ND - 1 else 0))
                b.wait("vector", "pe")
                b.wait("vector", "dy", x_back)
                b.op("vector", lambda eng, j=j, tt=tt, bk=bk:
                     eng.scalar_tensor_tensor(
                         self.x[:, tt * D + j * 256: tt * D + j * 256 + 256],
                         self.pb[bk][:, 0:256], 1.0,
                         self.x[:, tt * D + j * 256: tt * D + j * 256 + 256],
                         OP.mult, OP.add))
                self.bank_release(bk, "ve")
            ws.done()
        assert not ws.plan and not ws.dma_done



_CACHE = {}


def build_program(parts, load_x=True, store_x=True):
    b = B()
    mk = MK(b)
    mk.emit_prolog(load_x=load_x)
    for p in parts:
        if p == "ffn0":
            mk.emit_ffn0(p)
        elif p == "att0":
            mk.emit_att0(p)
        elif p == "att0_pre":
            mk.emit_att0(p, mode="pre")
        elif p == "att0_post":
            mk.emit_att0(p, mode="post")
        elif p == "ffn1":
            mk.emit_ffn1(p)
        elif p == "att1":
            mk.emit_att1(p)
        elif p == "att1_pre":
            mk.emit_att1(p, mode="pre")
        elif p == "att1_post":
            mk.emit_att1(p, mode="post")
        else:
            raise ValueError(p)
    if store_x:
        mk.emit_epilog()
    nc = b.finalize()
    return nc, mk


def _core_inputs(L, x_np):
    maps = []
    for c in range(NCORE):
        m = {}
        m["x_in"] = np.ascontiguousarray(x_np[c * T:(c + 1) * T])
        m["rope_cos"] = np.ascontiguousarray(L["_rope_cos_all"][:, c * T:(c + 1) * T])
        m["rope_ssin"] = np.ascontiguousarray(L["_rope_ssin_all"][:, c * T:(c + 1) * T])
        for k, v in L.items():
            if not k.startswith("_"):
                m[k] = v
        maps.append(m)
    return maps


FUSED = False

_STAGES = [
    (["ffn0", "att0_pre"],
     ["x_out", "a0k_loc", "a0v_loc", "a0q_dump"]),
    (["att0_post", "ffn0", "ffn1", "att1_pre"],
     ["x_out", "a1k_loc", "a1v_loc", "a1q_dump", "a1gt_dump"]),
    (["att1_post", "ffn1"], ["x_out"]),
]


def _launch(parts, L, x, extra):
    key = tuple(parts)
    if key not in _CACHE:
        _CACHE[key] = build_program(list(parts))
    nc, mk = _CACHE[key]
    needed = set(mk.input_names)
    in_maps = []
    for c, m in enumerate(_core_inputs(L, x)):
        mm = {k: v for k, v in m.items() if k in needed}
        for k, v in extra.items():
            if k in needed:
                mm[k] = v[c] if isinstance(v, list) else v
        in_maps.append(mm)
    missing = needed - set(in_maps[0].keys())
    assert not missing, missing
    return run_bass_kernel_spmd(nc, in_maps, core_ids=list(range(NCORE)))


def kernel(**inputs):
    L = host_layouts(inputs)
    x = np.ascontiguousarray(
        np.asarray(inputs["L0_attn_x"], np.float32).reshape(NCORE * T, D))

    if FUSED:
        res = _launch(("ffn0", "att0", "ffn0", "ffn1", "att1", "ffn1"),
                      L, x, {})
        out = np.concatenate([r["x_out"] for r in res.results], axis=0)
        return out.reshape(1, NCORE * T, D).astype(np.float32)

    extra = {}
    for parts, outs in _STAGES:
        res = _launch(tuple(parts), L, x, extra)
        x = np.ascontiguousarray(
            np.concatenate([r["x_out"] for r in res.results], axis=0))
        extra = {}
        if "a0k_loc" in outs:
            extra["a0k_all"] = np.ascontiguousarray(np.concatenate(
                [r["a0k_loc"] for r in res.results], axis=0))
            extra["a0v_all"] = np.ascontiguousarray(np.concatenate(
                [r["a0v_loc"] for r in res.results], axis=0))
            extra["a0q_dump"] = [np.ascontiguousarray(r["a0q_dump"])
                                 for r in res.results]
        if "a1k_loc" in outs:
            extra["a1k_all"] = np.ascontiguousarray(np.concatenate(
                [r["a1k_loc"] for r in res.results], axis=0))
            extra["a1v_all"] = np.ascontiguousarray(np.concatenate(
                [r["a1v_loc"] for r in res.results], axis=0))
            extra["a1q_dump"] = [np.ascontiguousarray(r["a1q_dump"])
                                 for r in res.results]
            extra["a1gt_dump"] = [np.ascontiguousarray(r["a1gt_dump"])
                                  for r in res.results]
    return x.reshape(1, NCORE * T, D).astype(np.float32)
